# version 20 (final)
# speedup vs baseline: 1.2431x; 1.0106x over previous
import numpy as np
import concourse.bass as bass
import concourse.mybir as mybir
from concourse.bass_utils import run_bass_kernel_spmd

F32 = mybir.dt.float32
BF16 = mybir.dt.bfloat16
AF = mybir.ActivationFunctionType
ALU = mybir.AluOpType
AX = mybir.AxisListType

NEG_BIG = -1.0e30


class Cfg:
    def __init__(self, T=4096, DFF=4096):
        self.DM = 1024
        self.T = T
        self.DFF = DFF
        self.RANK = 256
        self.NPAIR = 8
        self.OWN = T // 2
        self.NT = T // 128
        self.SLOTS = self.OWN // 128
        self.G = self.SLOTS // 4
        self.CCH = self.DM // 128
        self.FC = DFF // 128
        self.NT_OWN = self.OWN // 128
        assert self.SLOTS % 4 == 0 and self.OWN % 1024 == 0


class B:
    def __init__(self, cfg):
        self.cfg = cfg
        self.nc = bass.Bass("TRN2", target_bir_lowering=False)
        c = cfg
        dt = self.nc.dram_tensor
        self.d_xfT = dt("x_fullT", [c.DM, c.T], F32, kind="ExternalInput").ap()
        self.d_xo = dt("x_own", [c.OWN, c.DM], F32, kind="ExternalInput").ap()
        self.d_qkvwT = dt("qkv_wT", [c.DM, 3 * c.DM], F32, kind="ExternalInput").ap()
        self.d_qkvb = dt("qkv_b2d", [128, 3 * c.DM // 128], F32, kind="ExternalInput").ap()
        self.d_owT = dt("out_wT", [c.DM, c.DM], F32, kind="ExternalInput").ap()
        self.d_ob = dt("out_b2d", [1, c.DM], F32, kind="ExternalInput").ap()
        self.d_u1 = dt("U1", [c.DM, c.RANK], F32, kind="ExternalInput").ap()
        self.d_v1 = dt("V1", [c.RANK, c.DFF], F32, kind="ExternalInput").ap()
        self.d_b1 = dt("b1_2d", [128, c.FC], F32, kind="ExternalInput").ap()
        self.d_u2 = dt("U2", [c.DFF, c.RANK], F32, kind="ExternalInput").ap()
        self.d_v2 = dt("V2", [c.RANK, c.DM], F32, kind="ExternalInput").ap()
        self.d_b2 = dt("b2_2d", [1, c.DM], F32, kind="ExternalInput").ap()
        self.d_ident = dt("c_ident", [128, 128], F32, kind="ExternalInput").ap()
        self.d_masks = dt("c_masks", [128, 512], BF16, kind="ExternalInput").ap()
        self.d_y = dt("y_own", [c.OWN, c.DM], F32, kind="ExternalOutput").ap()
        self.d_attTh = dt("attTh_scr", [c.G, c.DM, 512], BF16, kind="Internal").ap()
        self.d_attTl = dt("attTl_scr", [c.G, c.DM, 512], BF16, kind="Internal").ap()

        def al(name, shape, dtype):
            return self.nc.alloc_sbuf_tensor(name, shape, dtype, side="right")
        self.ident = al("ident", [128, 128], F32).ap()
        self.masks = al("masks", [128, 512], BF16).ap()
        self.qkvb = al("qkvb", [128, 3 * c.DM // 128], F32).ap()
        self.mneg = al("mneg", [128, c.NPAIR], F32).ap()
        self.stat = al("stat", [128, 24], F32).ap()
        self.ident16 = al("ident16", [128, 128], BF16).ap()
        self.ones16 = al("ones16", [128, 128], BF16).ap()
        self.mneg16 = al("mneg16", [1, 16], BF16).ap()
        self.vb16s = al("vb16s", [1, 256], BF16).ap()
        self.A32 = 8832
        self.A16 = 87040
        self.ar32 = self.nc.alloc_sbuf_tensor("arena", [128, self.A32], F32).ap()
        self.ar16 = self.nc.alloc_sbuf_tensor("arena16", [128, self.A16], BF16).ap()
        self.off32 = 0
        self.off16 = 0

        self.psum = self.nc.alloc_psum_tensor("psum", [128, 4096], F32).ap()
        self.pb = [self.psum[:, 512 * i:512 * (i + 1)] for i in range(8)]
        self.bguard = {i: None for i in range(8)}

        self.sems = {}
        self.cnt = {}
        self.ops = {e: [] for e in ("sync", "tensor", "vector", "scalar", "gpsimd")}
        self._pend = {}
        self._semctx = []
        self.dpool = {"in": [f"s_i{j}" for j in range(4)],
                      "sp": [f"s_p{j}" for j in range(2)],
                      "y": [f"s_y{j}" for j in range(2)],
                      "gw": [f"s_g{j}" for j in range(2)]}
        self.drot = {k: 0 for k in self.dpool}
        semnames = ["s_t", "s_v", "s_a"]
        for pool in self.dpool.values():
            semnames += pool
        for s in semnames:
            ctx = self.nc.semaphore(s)
            self.sems[s] = ctx.__enter__()
            self._semctx.append(ctx)
            self.cnt[s] = 0

    def sub(self, n, inner=None):
        a = self.ar32[:, self.off32:self.off32 + n]
        self.off32 += n
        assert self.off32 <= self.A32, f"arena32 overflow {self.off32}"
        if inner is not None:
            a = a.rearrange("p (a b) -> p a b", b=inner)
        return a

    def sub16(self, n, inner=None):
        a = self.ar16[:, self.off16:self.off16 + n]
        self.off16 += n
        assert self.off16 <= self.A16, f"arena16 overflow {self.off16}"
        if inner is not None:
            a = a.rearrange("p (a b) -> p a b", b=inner)
        return a

    def _waitraw(self, eng, name, val):
        if val <= 0:
            return
        sem = self.sems[name]
        self.ops[eng].append(lambda e, sem=sem, val=val: e.wait_ge(sem, val))

    def emit(self, eng, fn):
        if eng == "vector":
            assert not self._pend.get(eng)
            self._waitraw(eng, "s_v", self.cnt["s_v"])
            self._pend[eng] = True
        elif eng == "scalar":
            assert not self._pend.get(eng)
            self._waitraw(eng, "s_a", self.cnt["s_a"])
            self._pend[eng] = True
        self.ops[eng].append(fn)

    def wait(self, eng, name, val):
        assert val <= self.cnt[name], f"wait-on-future {name} {val}>{self.cnt[name]}"
        self._waitraw(eng, name, val)

    def wait_g(self, eng, guard):
        if guard is not None:
            self.wait(eng, guard[0], guard[1])

    def inc_tag(self, eng, name, n=1):
        self.cnt[name] += n
        sem, val = self.sems[name], self.cnt[name]
        fn = self.ops[eng].pop()
        self.ops[eng].append(lambda e, fn=fn, sem=sem, n=n: fn(e).then_inc(sem, n))
        self._pend[eng] = False
        return val

    def dma(self, stream, fn, eng="sync"):
        pool = self.dpool[stream]
        j = self.drot[stream]
        self.drot[stream] = (j + 1) % len(pool)
        s = pool[j]
        self.wait(eng, s, self.cnt[s])
        self.emit(eng, fn)
        return (s, self.inc_tag(eng, s, 16))

    def wait_stream(self, eng, stream):
        for s in self.dpool[stream]:
            self.wait(eng, s, self.cnt[s])

    def mm3(self, out, lh, ll, rh, rl, start, stop):
        self.emit("tensor", lambda e, o=out, a=lh, b=rh, st=start:
                  e.matmul(o, a, b, start=st, stop=False))
        self.emit("tensor", lambda e, o=out, a=lh, b=rl:
                  e.matmul(o, a, b, start=False, stop=False))
        self.emit("tensor", lambda e, o=out, a=ll, b=rh, sp=stop:
                  e.matmul(o, a, b, start=False, stop=sp))
        return self.inc_tag("tensor", "s_t")

    def evac_b(self, src, hi, lo):
        self.emit("scalar", lambda e, src=src, hi=hi: e.activation(
            out=hi, in_=src, func=AF.Copy, bias=0.0, scale=1.0))
        av = self.inc_tag("scalar", "s_a")
        self.wait("vector", "s_a", av)
        self.emit("vector", lambda e, src=src, hi=hi, lo=lo: e.tensor_tensor(
            lo, src, hi, ALU.subtract))
        return av, self.inc_tag("vector", "s_v")

    def evac_a(self, src, tmp32, hi, lo, bias_ap):
        self.wait("scalar", "s_v", self.cnt["s_v"])
        self.emit("scalar", lambda e, src=src, t=tmp32, b=bias_ap: e.activation(
            out=t, in_=src, func=AF.Identity, bias=b, scale=1.0))
        av = self.inc_tag("scalar", "s_a")
        self.wait("vector", "s_a", av)
        self.emit("vector", lambda e, t=tmp32, hi=hi: e.tensor_copy(hi, t))
        self.inc_tag("vector", "s_v")
        self.emit("vector", lambda e, t=tmp32, hi=hi, lo=lo: e.tensor_tensor(
            lo, t, hi, ALU.subtract))
        return av, self.inc_tag("vector", "s_v")

    def build(self):
        c = self.cfg
        self.xTh = self.sub16(c.CCH * c.T, c.T)
        self.xTl = self.sub16(c.CCH * c.T, c.T)
        self.wth = self.sub16(3 * c.DM, 128)
        self.wtl = self.sub16(3 * c.DM, 128)
        self.qth = self.sub16(c.OWN)
        self.qtl = self.sub16(c.OWN)
        self.kth = self.sub16(c.T)
        self.ktl = self.sub16(c.T)
        self.atth_t = self.sub16(1024)
        self.rs16 = self.sub16(2048)
        self.vv = self.sub(c.NT * 130, 130)
        self.at_flat = self.sub(3072)
        self.at = self.at_flat.rearrange("p (a b) -> p a b", b=512)
        self.wbuf = self.sub(1536)
        self.rs = self.wbuf[:, 0:1024]
        att_end = (self.off32, self.off16)

        self.prep()
        self.prefetch_whi(0)
        for p in range(c.NPAIR):
            self.pair_proj(p)
            self.pair_m(p)
            if p + 1 < c.NPAIR:
                self.prefetch_whi(p + 1)
            self.pair_attn(p)

        self.off32 = 0
        self.off16 = 0
        self.x1tmh = self.sub16(c.NT_OWN * c.DM, c.DM)
        self.x1tml = self.sub16(c.NT_OWN * c.DM, c.DM)
        self.x1Th = self.sub16(c.CCH * c.OWN, c.OWN)
        self.x1Tl = self.sub16(c.CCH * c.OWN, c.OWN)
        ph34 = self.off16
        self.woTh = self.sub16(c.CCH * c.DM, c.DM)
        self.woTl = self.sub16(c.CCH * c.DM, c.DM)
        self.asbh = self.sub16(c.CCH * 128, 128)
        self.asbl = self.sub16(c.CCH * 128, 128)
        self.xo = self.sub(2 * c.DM, c.DM)
        self.lnx = self.sub(c.DM)
        self.lnsq = self.sub(c.DM)
        self.x1t_tmp = self.sub(2 * c.DM, c.DM)
        self.wstage = self.sub(c.DM)
        self.obs = self.sub(c.DM)
        self.obs16 = self.sub16(2048)
        p3_end = (self.off32, self.off16)
        self.phase3()

        self.off16 = ph34
        self.off32 = 0
        self.lnx4 = self.sub(c.DM)
        self.lnsq4 = self.sub(c.DM)
        self.u1sh = self.sub16(c.CCH * c.RANK, c.RANK)
        self.u1sl = self.sub16(c.CCH * c.RANK, c.RANK)
        self.v2sh = self.sub16(2 * c.DM, c.DM)
        self.v2sl = self.sub16(2 * c.DM, c.DM)
        self.wsplit = self.sub(c.DM)
        self.v1s32 = self.sub(2 * 256, 256)
        self.u2s32 = self.sub(2 * 256, 256)
        self.v1sth = self.sub16(2 * 256, 256)
        self.v1stl = self.sub16(2 * 256, 256)
        self.u2sth = self.sub16(2 * 256, 256)
        self.u2stl = self.sub16(2 * 256, 256)
        self.pth = self.sub16(2 * 1024, 1024)
        self.ptl = self.sub16(2 * 1024, 1024)
        self.htsh = self.sub16(2 * 512, 512)
        self.htsl = self.sub16(2 * 512, 512)
        self.gtmp2 = self.wsplit.rearrange("p (a b) -> p a b", b=512)
        self.q2th = self.sub16(2 * 512, 512)
        self.q2tl = self.sub16(2 * 512, 512)
        self.ytile = self.sub(2 * c.DM, c.DM)
        self.b1s = self.sub(c.FC)
        self.b2s = self.sub(c.DM)
        self.b2s16 = self.sub16(2048)
        p4_end = (self.off32, self.off16)
        self.phase4()
        self.sbuf_free = (att_end, p3_end, p4_end)

        with self.nc.Block() as block:
            @block.sync
            def _(e):
                for fn in self.ops["sync"]:
                    fn(e)

            @block.tensor
            def _(e):
                for fn in self.ops["tensor"]:
                    fn(e)

            @block.vector
            def _(e):
                for fn in self.ops["vector"]:
                    fn(e)

            @block.scalar
            def _(e):
                for fn in self.ops["scalar"]:
                    fn(e)

            @block.gpsimd
            def _(e):
                for fn in self.ops["gpsimd"]:
                    fn(e)
        for ctx in self._semctx:
            ctx.__exit__(None, None, None)
        return self.nc

    def transpose_block_hl(self, src, dst_fn, nchunks=8, banks=(6, 7)):
        tlast = 0
        for ch in range(nchunks):
            bi = banks[ch % len(banks)]
            bank = self.pb[bi]
            self.wait_g("tensor", self.bguard[bi])
            self.emit("tensor", lambda e, src=src, ch=ch, bank=bank: e.transpose(
                bank[:, 0:128], src[:, ch * 128:(ch + 1) * 128], self.ident))
            tlast = self.inc_tag("tensor", "s_t")
            hi, lo = dst_fn(ch)
            self.wait("scalar", "s_t", tlast)
            _, v = self.evac_b(bank[:, 0:128], hi, lo)
            self.bguard[bi] = ("s_v", v)
        return tlast

    def prep(self):
        c = self.cfg
        g_id = self.dma("in", lambda e: e.dma_start(out=self.ident, in_=self.d_ident))
        self.g_masks = self.dma("in", lambda e: e.dma_start(out=self.masks, in_=self.d_masks))
        self.g_qkvb = self.dma("in", lambda e: e.dma_start(out=self.qkvb, in_=self.d_qkvb))
        self.emit("vector", lambda e: e.memset(self.stat[:, 16:17], 1.0e-5))
        self.inc_tag("vector", "s_v")
        self.wait_g("vector", g_id)
        self.emit("vector", lambda e: e.tensor_copy(self.ident16, self.ident))
        self.inc_tag("vector", "s_v")
        self.emit("vector", lambda e: e.memset(self.ones16, 1.0))
        self.inc_tag("vector", "s_v")
        self.wait_g("tensor", g_id)
        xfT = self.d_xfT.rearrange("(a p) t -> p a t", p=128)
        nseg = c.T // 1024
        stgs = [self.at_flat[:, 0:1024], self.at_flat[:, 1024:2048],
                self.at_flat[:, 2048:3072], self.wbuf[:, 0:1024]]
        stg_guard = [None, None, None, None]
        for ch in range(c.CCH):
            for sg in range(nseg):
                dbi = (nseg * ch + sg) % 4
                qeng = "sync" if dbi < 2 else "gpsimd"
                pool = "in" if dbi < 2 else "gw"
                stg = stgs[dbi]
                self.wait_g(qeng, stg_guard[dbi])
                gx = self.dma(pool, lambda e, ch=ch, sg=sg, stg=stg: e.dma_start(
                    out=stg, in_=xfT[:, ch, 1024 * sg:1024 * (sg + 1)]), eng=qeng)
                self.wait_g("scalar", gx)
                self.emit("scalar", lambda e, ch=ch, sg=sg, stg=stg: e.activation(
                    out=self.xTh[:, ch, 1024 * sg:1024 * (sg + 1)], in_=stg,
                    func=AF.Copy, bias=0.0, scale=1.0))
                ah = self.inc_tag("scalar", "s_a")
                self.wait("vector", "s_a", ah)
                self.emit("vector", lambda e, ch=ch, sg=sg, stg=stg:
                          e.tensor_tensor(self.xTl[:, ch, 1024 * sg:1024 * (sg + 1)],
                                          stg,
                                          self.xTh[:, ch, 1024 * sg:1024 * (sg + 1)],
                                          ALU.subtract))
                vsp = self.inc_tag("vector", "s_v")
                stg_guard[dbi] = ("s_v", vsp)

    def prefetch_whi(self, p):
        c = self.cfg
        qkvwT = self.d_qkvwT.rearrange("(a p) o -> p a o", p=128)
        self.wait("gpsimd", "s_t", self.cnt["s_t"])
        for k, col0 in enumerate((128 * p, c.DM + 128 * p, 2 * c.DM + 128 * p)):
            self.dma("gw", lambda e, col0=col0, k=k: e.dma_start(
                out=self.wth[:, k * c.CCH:(k + 1) * c.CCH, :],
                in_=qkvwT[:, :, col0:col0 + 128]), eng="gpsimd")

    def pair_proj(self, p):
        c = self.cfg
        pend = getattr(self, "_attn_pending", None)
        pend_vr = None
        if pend is not None:
            pp, (pg, pob, pt) = pend
            pend_vr = self.evac_v(pp, pg, pob, pt)
        self.wait_g("sync", getattr(self, "last_sp", None))
        self.wait_g("sync", getattr(self, "last_sp2", None))
        self.wait("sync", "s_t", self.cnt["s_t"])
        self.wait("sync", "s_v", self.cnt["s_v"])
        if p == 0:
            self.wait_g("scalar", self.g_qkvb)
            self.wait_g("tensor", self.g_qkvb)

        qkvwT = self.d_qkvwT.rearrange("(a p) o -> p a o", p=128)
        for s in self.dpool["gw"]:
            self.wait("vector", s, self.cnt[s])

        def load_w(col0, base, st, qeng="sync", pool="in"):
            gw = self.dma(pool, lambda e, st=st, col0=col0: e.dma_start(
                out=st.rearrange("p (a b) -> p a b", b=128),
                in_=qkvwT[:, :, col0:col0 + 128]), eng=qeng)
            self.wait_g("vector", gw)
            self.emit("vector", lambda e, st=st, base=base: e.tensor_tensor(
                self.wtl[:, base:base + c.CCH, :].rearrange("p a b -> p (a b)"), st,
                self.wth[:, base:base + c.CCH, :].rearrange("p a b -> p (a b)"),
                ALU.subtract))
            return self.inc_tag("vector", "s_v")

        self.wait("gpsimd", "s_t", self.cnt["s_t"])
        load_w(128 * p, 0, self.at_flat[:, 0:1024])
        load_w(c.DM + 128 * p, c.CCH, self.at_flat[:, 1024:2048], "gpsimd", "gw")
        load_w(2 * c.DM + 128 * p, 2 * c.CCH, self.at_flat[:, 2048:3072])
        self.wait("tensor", "s_v", self.cnt["s_v"])
        for s in self.dpool["gw"]:
            self.wait("tensor", s, self.cnt[s])

        self.wait("vector", "s_t", self.cnt["s_t"])
        self.emit("vector", lambda e: e.memset(self.vv[:, :, 64:65], 1.0))
        self.inc_tag("vector", "s_v")
        self.emit("vector", lambda e: e.memset(self.vv[:, :, 129:130], 1.0))
        self.inc_tag("vector", "s_v")
        self.vb16 = self.vb16s
        bvb = self.pb[3]
        self.wait_g("tensor", self.bguard[3])
        self.emit("tensor", lambda e, p=p, bvb=bvb: e.transpose(
            bvb[0:1, 0:128], self.qkvb[:, 2 * c.CCH + p:2 * c.CCH + p + 1], self.ident))
        tvb = self.inc_tag("tensor", "s_t")
        self.wait("vector", "s_t", tvb)
        self.wait_g("vector", getattr(self, "last_sp", None))
        self.emit("vector", lambda e: e.tensor_copy(self.vb16[0:1, 0:128], bvb[0:1, 0:128]))
        self.inc_tag("vector", "s_v")
        self.emit("vector", lambda e: e.tensor_tensor(
            self.vb16[0:1, 128:256], bvb[0:1, 0:128],
            self.vb16[0:1, 0:128], ALU.subtract))
        vvb = self.inc_tag("vector", "s_v")
        self.bguard[3] = ("s_v", vvb)

        self.wait("tensor", "s_v", self.cnt["s_v"])
        self.wait_g("vector", getattr(self, "last_sp", None))
        self.wait_g("vector", getattr(self, "last_sp2", None))
        tmp32 = self.at[:, 1, :]

        norm_pend = []
        if not hasattr(self, "_norm_slot_g"):
            self._norm_slot_g = [None, None]

        def norm_square(src_hi, off, col):
            sq = self.atth_t[:, 512 * (col % 2):512 * (col % 2) + 512]
            self.wait_g("vector", self._norm_slot_g[col % 2])
            self.wait_g("vector", getattr(self, "last_sp", None))
            self.wait_g("vector", getattr(self, "last_sp2", None))
            self.emit("vector", lambda e, src_hi=src_hi, off=off, sq=sq: e.tensor_mul(
                sq, src_hi[:, off:off + 512], src_hi[:, off:off + 512]))
            sqv = self.inc_tag("vector", "s_v")
            norm_pend.append((sq, sqv, col))

        def norm_mm_flush(keep):
            while len(norm_pend) > keep:
                sq, sqv, col = norm_pend.pop(0)
                bi = 4 + (col % 2)
                bank = self.pb[bi]
                self.wait("tensor", "s_v", sqv)
                self.wait_g("tensor", self.bguard[bi])
                self.emit("tensor", lambda e, bank=bank, sq=sq: e.matmul(
                    bank[:, :], self.ones16[:, 0:128], sq, start=True, stop=True))
                tv = self.inc_tag("tensor", "s_t")
                self._norm_slot_g[col % 2] = ("s_t", tv)
                self.wait("vector", "s_t", tv)
                self.emit("vector", lambda e, bank=bank, col=col: e.reduce_max(
                    self.stat[0:1, col:col + 1], bank[0:1, :], axis=AX.X))
                v = self.inc_tag("vector", "s_v")
                self.bguard[bi] = ("s_v", v)
        for tc in range(c.OWN // 512):
            bi = tc % 4
            bank = self.pb[bi]
            self.wait_g("tensor", self.bguard[bi])
            for ch in range(c.CCH):
                rh = self.xTh[:, ch, 1024 * tc:1024 * (tc + 1)].rearrange(
                    "p (s j) -> p s j", j=256)[:, :, 0:128]
                rl = self.xTl[:, ch, 1024 * tc:1024 * (tc + 1)].rearrange(
                    "p (s j) -> p s j", j=256)[:, :, 0:128]
                ob = bank.rearrange("p (s j) -> p s j", j=128)
                self.mm3(ob, self.wth[:, ch, :], self.wtl[:, ch, :], rh, rl,
                         start=(ch == 0), stop=(ch == c.CCH - 1))
            self.wait("scalar", "s_t", self.cnt["s_t"])
            _, v = self.evac_a(bank[:, :], tmp32,
                               self.qth[:, 512 * tc:512 * (tc + 1)],
                               self.qtl[:, 512 * tc:512 * (tc + 1)],
                               self.qkvb[:, p:p + 1])
            self.bguard[bi] = ("s_v", v)
            norm_mm_flush(1)
            norm_square(self.qth, 512 * tc, tc)

        norm_mm_flush(0)
        if pend is not None:
            pp, (pg, pob, pt) = pend
            self.evac_t(pp, pg, pob, pend_vr, (4, 5))
            self._attn_pending = None
        for tc in range(c.T // 512):
            bi = tc % 4
            bank = self.pb[bi]
            self.wait_g("tensor", self.bguard[bi])
            for ch in range(c.CCH):
                self.mm3(bank[:, :], self.wth[:, c.CCH + ch, :], self.wtl[:, c.CCH + ch, :],
                         self.xTh[:, ch, 512 * tc:512 * (tc + 1)],
                         self.xTl[:, ch, 512 * tc:512 * (tc + 1)],
                         start=(ch == 0), stop=(ch == c.CCH - 1))
            self.wait("scalar", "s_t", self.cnt["s_t"])
            _, v = self.evac_a(bank[:, :], tmp32,
                               self.kth[:, 512 * tc:512 * (tc + 1)],
                               self.ktl[:, 512 * tc:512 * (tc + 1)],
                               self.qkvb[:, c.CCH + p:c.CCH + p + 1])
            self.bguard[bi] = ("s_v", v)
            norm_mm_flush(1)
            norm_square(self.kth, 512 * tc, c.OWN // 512 + tc)

        for tt in range(c.NT):
            bi = 2 + (tt % 4)
            bank = self.pb[bi]
            self.wait_g("tensor", self.bguard[bi])
            for ch in range(c.CCH):
                self.mm3(bank[:, 0:128],
                         self.xTh[:, ch, 128 * tt:128 * (tt + 1)],
                         self.xTl[:, ch, 128 * tt:128 * (tt + 1)],
                         self.wth[:, 2 * c.CCH + ch, :],
                         self.wtl[:, 2 * c.CCH + ch, :],
                         start=(ch == 0), stop=False)
            self.emit("tensor", lambda e, bank=bank: e.matmul(
                bank[:, 0:128], self.ones16[0:1, :],
                self.vb16[0:1, 0:128], start=False, stop=False))
            self.emit("tensor", lambda e, bank=bank: e.matmul(
                bank[:, 0:128], self.ones16[0:1, :],
                self.vb16[0:1, 128:256], start=False, stop=True))
            tmm = self.inc_tag("tensor", "s_t")
            self.wait("scalar", "s_t", tmm)
            self.emit("scalar", lambda e, bank=bank, tt=tt: e.activation(
                out=self.vv[:, tt, 0:64], in_=bank[:, 0:64], func=AF.Copy,
                bias=0.0, scale=1.0))
            self.inc_tag("scalar", "s_a")
            self.emit("scalar", lambda e, bank=bank, tt=tt: e.activation(
                out=self.vv[:, tt, 65:129], in_=bank[:, 64:128], func=AF.Copy,
                bias=0.0, scale=1.0))
            a_ev = self.inc_tag("scalar", "s_a")
            self.bguard[bi] = ("s_a", a_ev)
        norm_mm_flush(0)
        self._proj_a = self.cnt["s_a"]
        self._proj_v = self.cnt["s_v"]

    def pair_m(self, p):
        c = self.cfg
        nq = c.OWN // 512
        nk = c.T // 512
        self.emit("vector", lambda e, nq=nq: e.reduce_max(
            self.stat[0:1, 18:19], self.stat[0:1, 0:nq], axis=AX.X))
        self.inc_tag("vector", "s_v")
        self.emit("vector", lambda e, nq=nq, nk=nk: e.reduce_max(
            self.stat[0:1, 19:20], self.stat[0:1, nq:nq + nk], axis=AX.X))
        vmax = self.inc_tag("vector", "s_v")
        self.wait("scalar", "s_v", vmax)
        self.emit("scalar", lambda e: e.activation(
            out=self.stat[0:1, 20:21], in_=self.stat[0:1, 18:19], func=AF.Sqrt,
            bias=0.0, scale=1.0))
        self.inc_tag("scalar", "s_a")
        self.emit("scalar", lambda e: e.activation(
            out=self.stat[0:1, 21:22], in_=self.stat[0:1, 19:20], func=AF.Sqrt,
            bias=0.0, scale=1.0))
        sa = self.inc_tag("scalar", "s_a")
        self.wait("vector", "s_a", sa)
        self.emit("vector", lambda e: e.tensor_mul(
            self.stat[0:1, 18:19], self.stat[0:1, 20:21], self.stat[0:1, 21:22]))
        vm = self.inc_tag("vector", "s_v")
        self.wait("scalar", "s_v", vm)
        self.emit("scalar", lambda e, p=p: e.activation(
            out=self.mneg16[0:1, p:p + 1], in_=self.stat[0:1, 18:19], func=AF.Copy,
            bias=0.0, scale=-0.13))
        sa = self.inc_tag("scalar", "s_a")
        bi = 6
        bank = self.pb[bi]
        self.wait("tensor", "s_a", sa)
        self.wait_g("tensor", self.bguard[bi])
        self.emit("tensor", lambda e, bank=bank, p=p: e.matmul(
            bank[:, 0:1], self.ones16[0:1, :], self.mneg16[0:1, p:p + 1],
            start=True, stop=True))
        tv = self.inc_tag("tensor", "s_t")
        self.wait("vector", "s_t", tv)
        self.emit("vector", lambda e, p=p, bank=bank: e.tensor_copy(
            self.mneg[:, p:p + 1], bank[:, 0:1]))
        v = self.inc_tag("vector", "s_v")
        self.bguard[bi] = ("s_v", v)
        self._mneg_v = v

    def evac_v(self, p, g, o_banks, t_flush):
        self.wait("vector", "s_t", t_flush)
        vr = 0
        for h in range(2):
            self.emit("vector", lambda e, h=h, ob=o_banks[h]: e.reciprocal(
                self.rs[0:1, 512 * h:512 * h + 512], ob[64:65, 0:512]))
            self.inc_tag("vector", "s_v")
            self.emit("vector", lambda e, h=h: e.tensor_copy(
                self.rs16[0:1, 512 * h:512 * h + 512],
                self.rs[0:1, 512 * h:512 * h + 512]))
            self.inc_tag("vector", "s_v")
            self.emit("vector", lambda e, h=h: e.tensor_tensor(
                self.rs16[0:1, 1024 + 512 * h:1024 + 512 * h + 512],
                self.rs[0:1, 512 * h:512 * h + 512],
                self.rs16[0:1, 512 * h:512 * h + 512], ALU.subtract))
            vr = self.inc_tag("vector", "s_v")
        return vr

    def evac_t(self, p, g, o_banks, vr, rbc_banks):
        attT32 = self.wbuf[:, 0:512]
        attT_h = self.atth_t[:, 0:512]
        attT_l = self.atth_t[:, 512:1024]
        rbc = self.wbuf[:, 512:1024]
        for h in range(2):
            rbi = rbc_banks[h]
            rb = self.pb[rbi]
            self.wait("tensor", "s_v", vr)
            self.wait_g("tensor", self.bguard[rbi])
            self.emit("tensor", lambda e, rb=rb, h=h: e.matmul(
                rb[:, 0:512], self.ones16[0:1, :],
                self.rs16[0:1, 512 * h:512 * h + 512], start=True, stop=False))
            self.emit("tensor", lambda e, rb=rb, h=h: e.matmul(
                rb[:, 0:512], self.ones16[0:1, :],
                self.rs16[0:1, 1024 + 512 * h:1024 + 512 * h + 512],
                start=False, stop=True))
            trb = self.inc_tag("tensor", "s_t")
            self.wait("vector", "s_t", trb)
            if h == 0:
                self.wait_g("vector", getattr(self, "last_sp", None))
                self.wait_g("vector", getattr(self, "last_sp2", None))
            self.emit("vector", lambda e, rb=rb, rbc=rbc: e.tensor_copy(
                rbc[0:64, :], rb[0:64, 0:512]))
            vv_ = self.inc_tag("vector", "s_v")
            self.bguard[rbi] = ("s_v", vv_)
            self.emit("vector", lambda e, h=h, ob=o_banks[h], at_t=attT32, rbc=rbc:
                      e.tensor_tensor(at_t[64 * h:64 * h + 64, :],
                                      ob[0:64, 0:512], rbc[0:64, :], ALU.mult))
            v = self.inc_tag("vector", "s_v")
            self.bguard[(4, 5)[h] if g % 2 == 0 else (6, 7)[h]] = ("s_v", v)
        self.emit("vector", lambda e, at_t=attT32, ah=attT_h: e.tensor_copy(ah, at_t))
        self.inc_tag("vector", "s_v")
        self.emit("vector", lambda e, at_t=attT32, ah=attT_h, al_=attT_l:
                  e.tensor_tensor(al_, at_t, ah, ALU.subtract))
        self.inc_tag("vector", "s_v")
        self.wait("sync", "s_v", self.cnt["s_v"])
        self.last_sp = self.dma("sp", lambda e, p=p, g=g, ah=attT_h: e.dma_start(
            out=self.d_attTh[g, 128 * p:128 * (p + 1), :], in_=ah))
        self.last_sp2 = self.dma("sp", lambda e, p=p, g=g, al_=attT_l: e.dma_start(
            out=self.d_attTl[g, 128 * p:128 * (p + 1), :], in_=al_))

    def pair_attn(self, p):
        c = self.cfg
        attT32 = self.wbuf[:, 0:512]
        attT_h = self.atth_t[:, 0:512]
        attT_l = self.atth_t[:, 512:1024]
        rbc = self.wbuf[:, 512:1024]
        at_guard = {}
        exp_val = {}
        self.wait("tensor", "s_a", self._proj_a)
        self.wait("tensor", "s_v", self._proj_v)
        if p == 0:
            self.wait_g("vector", self.g_masks)
        mneg_waited = [False]

        def emit_qk_exp(g, cc):
            s0 = max(4 * g, cc // 2)
            qoff = 128 * (s0 - 4 * g)
            par = cc % 2
            tqk = 0
            for h in range(2):
                sbi = 2 * h + par
                sb = self.pb[sbi]
                self.wait_g("tensor", self.bguard[sbi])
                tqk = self.mm3(
                    sb[:, qoff:512],
                    self.kth[64 * h:64 * h + 64, cc * 128:(cc + 1) * 128],
                    self.ktl[64 * h:64 * h + 64, cc * 128:(cc + 1) * 128],
                    self.qth[64 * h:64 * h + 64, 512 * g + qoff:512 * (g + 1)],
                    self.qtl[64 * h:64 * h + 64, 512 * g + qoff:512 * (g + 1)],
                    start=True, stop=True)
            s2 = self.psum[:, 512 * par:512 * par + 1536].rearrange(
                "p (a b) -> p a b", b=512)[:, 0:3:2, :]
            masked = (cc >= 8 * g)
            if masked:
                wcol = 128 * (cc // 2 - 4 * g)
                self.wait("vector", "s_t", tqk)
                self.emit("vector", lambda e, s2=s2, wcol=wcol, par=par:
                          e.tensor_tensor(
                              s2[:, :, wcol:wcol + 128], s2[:, :, wcol:wcol + 128],
                              self.masks[:, 256 * par:256 * par + 256].rearrange(
                                  "p (a b) -> p a b", b=128),
                              ALU.add))
                vmask = self.inc_tag("vector", "s_v")
                self.wait("scalar", "s_v", vmask)
            else:
                self.wait("scalar", "s_t", tqk)
            if not mneg_waited[0]:
                self.wait("scalar", "s_v", self._mneg_v)
                mneg_waited[0] = True
            self.wait_g("scalar", at_guard.get((0, cc % 3)))
            self.wait_g("scalar", at_guard.get((1, cc % 3)))
            at2 = self.at[:, cc % 3:cc % 3 + 4:3, qoff:512]
            self.emit("scalar", lambda e, s2=s2, at2=at2, qoff=qoff, p=p:
                      e.activation(out=at2, in_=s2[:, :, qoff:512], func=AF.Exp,
                                   bias=self.mneg[:, p:p + 1], scale=0.125))
            aexp = self.inc_tag("scalar", "s_a")
            for h in range(2):
                exp_val[(h, cc % 3)] = aexp
                self.bguard[2 * h + par] = ("s_a", aexp)

        def emit_av(g, cc, o_banks, ob_idx):
            s0 = max(4 * g, cc // 2)
            qoff = 128 * (s0 - 4 * g)
            par = cc % 2
            for h in range(2):
                self.wait("tensor", "s_a", exp_val[(h, cc % 3)])
                if cc == 0:
                    self.wait_g("tensor", self.bguard[ob_idx[h]])
                self.emit("tensor", lambda e, h=h, sl=cc % 3, cc=cc, qoff=qoff,
                          ob=o_banks[h], g=g:
                          e.matmul(ob[0:65, qoff:512],
                                   self.vv[:, cc, 65 * h:65 * h + 65],
                                   self.at[:, 3 * h + sl, qoff:512],
                                   start=(cc == 0), stop=(cc == 8 * g + 7)))
                at_guard[(h, cc % 3)] = ("s_t", self.inc_tag("tensor", "s_t"))

        def o_banks_idx(g):
            return (4, 5) if g % 2 == 0 else (6, 7)

        pending = getattr(self, "_attn_pending", None)
        assert pending is None
        for g in range(c.G):
            cmax = 8 * g + 8
            ob_idx = o_banks_idx(g)
            o_banks = (self.pb[ob_idx[0]], self.pb[ob_idx[1]])
            vr = None
            for cc in range(cmax):
                emit_qk_exp(g, cc)
                if cc == 1 and pending is not None:
                    pg, pob, pt = pending
                    vr = self.evac_v(p, pg, pob, pt)
                if cc == 7 and pending is not None:
                    pg, pob, pt = pending
                    rpar = (cc - 1) % 2
                    self.evac_t(p, pg, pob, vr, (0 + rpar, 2 + rpar))
                    pending = None
                if cc >= 2:
                    emit_av(g, cc - 2, o_banks, ob_idx)
            emit_av(g, cmax - 2, o_banks, ob_idx)
            emit_av(g, cmax - 1, o_banks, ob_idx)
            pending = (g, o_banks, self.cnt["s_t"])
        self._attn_pending = (p, pending)

    def phase3(self):
        c = self.cfg
        pend = getattr(self, "_attn_pending", None)
        if pend is not None:
            pp, (pg, pob, pt) = pend
            vr = self.evac_v(pp, pg, pob, pt)
            self.evac_t(pp, pg, pob, vr, (0, 2))
            self._attn_pending = None
        self.wait("sync", "s_t", self.cnt["s_t"])
        self.wait("sync", "s_a", self.cnt["s_a"])
        self.wait("sync", "s_v", self.cnt["s_v"])
        self.wait_stream("sync", "sp")
        self.wait("gpsimd", "s_t", self.cnt["s_t"])
        wstgs = [self.wstage, self.lnx, self.x1t_tmp[:, 0, :], self.x1t_tmp[:, 1, :]]
        wstg_guard = [None, None, None, None]
        for ch in range(c.CCH):
            k = ch % 4
            qeng = "sync" if k < 2 else "gpsimd"
            pool = "in" if k < 2 else "gw"
            stg = wstgs[k]
            self.wait_g(qeng, wstg_guard[k])
            gw = self.dma(pool, lambda e, ch=ch, stg=stg: e.dma_start(
                out=stg, in_=self.d_owT[128 * ch:128 * (ch + 1), :]), eng=qeng)
            self.wait_g("vector", gw)
            self.emit("vector", lambda e, ch=ch, stg=stg: e.tensor_copy(
                self.woTh[:, ch, :], stg))
            self.inc_tag("vector", "s_v")
            self.emit("vector", lambda e, ch=ch, stg=stg: e.tensor_tensor(
                self.woTl[:, ch, :], stg, self.woTh[:, ch, :], ALU.subtract))
            vsp = self.inc_tag("vector", "s_v")
            wstg_guard[k] = ("s_v", vsp)
        g_obs = self.dma("in", lambda e: e.dma_start(out=self.obs[0:1, :], in_=self.d_ob))
        self.wait_g("vector", g_obs)
        self.emit("vector", lambda e: e.tensor_copy(self.obs16[0:1, 0:1024], self.obs[0:1, :]))
        self.inc_tag("vector", "s_v")
        self.emit("vector", lambda e: e.tensor_tensor(
            self.obs16[0:1, 1024:2048], self.obs[0:1, :], self.obs16[0:1, 0:1024],
            ALU.subtract))
        self.inc_tag("vector", "s_v")
        self.wait("tensor", "s_v", self.cnt["s_v"])

        asb_guard = [None, None]
        xo_guard = [None, None]
        asbbufs = [(self.asbh, self.asbl),
                   (self.x1Th[:, :, c.OWN - 128:c.OWN],
                    self.x1Tl[:, :, c.OWN - 128:c.OWN])]

        def issue_asb(ti):
            g, w, dbi = ti // 4, ti % 4, ti % 2
            ah, al_ = asbbufs[dbi]
            self.wait_g("sync", asb_guard[dbi])
            ga = self.dma("in", lambda e, g=g, w=w, ah=ah: e.dma_start(
                out=ah, in_=self.d_attTh[g, :, 128 * w:128 * (w + 1)].rearrange(
                    "(a p) b -> p a b", p=128)))
            gb = self.dma("in", lambda e, g=g, w=w, al_=al_: e.dma_start(
                out=al_, in_=self.d_attTl[g, :, 128 * w:128 * (w + 1)].rearrange(
                    "(a p) b -> p a b", p=128)))
            return ga, gb

        def issue_xo(ti):
            dbi = ti % 2
            self.wait_g("gpsimd", xo_guard[dbi])
            return self.dma("gw", lambda e, ti=ti, dbi=dbi: e.dma_start(
                out=self.xo[:, dbi, :], in_=self.d_xo[128 * ti:128 * (ti + 1), :]),
                eng="gpsimd")

        self.wait("gpsimd", "s_t", self.cnt["s_t"])
        pf = {0: issue_asb(0), 1: issue_asb(1)}
        gxs = {0: issue_xo(0), 1: issue_xo(1)}
        for ti in range(c.NT_OWN):
            g, w, dbi = ti // 4, ti % 4, ti % 2
            ah, al_ = asbbufs[dbi]
            ga, gb = pf.pop(ti)
            gx = gxs.pop(ti)
            self.wait_g("tensor", ga)
            self.wait_g("tensor", gb)
            self.wait_g("tensor", gx)
            tmm = 0
            for dk in range(2):
                ybi = (2 * ti + dk) % 4
                bank = self.pb[ybi]
                self.wait_g("tensor", self.bguard[ybi])
                for ch in range(c.CCH):
                    self.mm3(bank[:, :], ah[:, ch, :], al_[:, ch, :],
                             self.woTh[:, ch, 512 * dk:512 * (dk + 1)],
                             self.woTl[:, ch, 512 * dk:512 * (dk + 1)],
                             start=(ch == 0), stop=False)
                self.emit("tensor", lambda e, bank=bank, dk=dk:
                          e.matmul(bank[:, :], self.ones16[0:1, :],
                                   self.obs16[0:1, 512 * dk:512 * (dk + 1)],
                                   start=False, stop=False))
                self.emit("tensor", lambda e, bank=bank, dk=dk:
                          e.matmul(bank[:, :], self.ones16[0:1, :],
                                   self.obs16[0:1, 1024 + 512 * dk:1024 + 512 * (dk + 1)],
                                   start=False, stop=True))
                tmm = self.inc_tag("tensor", "s_t")
                self.wait("vector", "s_t", tmm)
                self.emit("vector", lambda e, bank=bank, dk=dk, dbi=dbi, lnx=self.lnx:
                          e.tensor_tensor(lnx[:, 512 * dk:512 * (dk + 1)],
                                          bank[:, :], self.xo[:, dbi, 512 * dk:512 * (dk + 1)],
                                          ALU.add))
                v = self.inc_tag("vector", "s_v")
                self.bguard[ybi] = ("s_v", v)
            asb_guard[dbi] = ("s_t", tmm)
            if ti + 2 < c.NT_OWN:
                pf[ti + 2] = issue_asb(ti + 2)
            xo_guard[dbi] = ("s_v", self.cnt["s_v"])
            if ti + 2 < c.NT_OWN:
                gxs[ti + 2] = issue_xo(ti + 2)
            if ti > 0:
                self.emit_x1T_transposes(ti - 1)
            tmp = self.x1t_tmp[:, dbi, :]
            vn = self.layernorm(self.lnx, tmp, ti, None, sq=self.lnsq)
            self._x1t_vn_map = getattr(self, "_x1t_vn_map", {})
            self._x1t_vn_map[ti] = vn
            self.emit("vector", lambda e, ti=ti, tmp=tmp: e.tensor_copy(
                self.x1tmh[:, ti, :], tmp))
            self.inc_tag("vector", "s_v")
            self.emit("vector", lambda e, ti=ti, tmp=tmp: e.tensor_tensor(
                self.x1tml[:, ti, :], tmp, self.x1tmh[:, ti, :],
                ALU.subtract))
            self.inc_tag("vector", "s_v")
        self.emit_x1T_transposes(c.NT_OWN - 1)

    def emit_x1T_transposes(self, ti):
        c = self.cfg
        self.wait("tensor", "s_v", self._x1t_vn_map[ti])
        self.transpose_block_hl(
            self.x1t_tmp[:, ti % 2, :],
            lambda ch, ti=ti: (self.x1Th[:, ch, 128 * ti:128 * (ti + 1)],
                               self.x1Tl[:, ch, 128 * ti:128 * (ti + 1)]),
            banks=(4, 5, 6, 7))

    def layernorm(self, xap, out_tm, ti, to_T, sq):
        c = self.cfg
        st = self.stat
        self.emit("vector", lambda e, xap=xap: e.reduce_sum(st[:, 0:1], xap, axis=AX.X))
        self.inc_tag("vector", "s_v")
        self.emit("vector", lambda e: e.tensor_scalar_mul(st[:, 1:2], st[:, 0:1], -1.0 / c.DM))
        self.inc_tag("vector", "s_v")
        v1 = self.emit("vector", lambda e: e.tensor_scalar_mul(st[:, 2:3], st[:, 0:1], 1.0 / c.DM))
        v1 = self.inc_tag("vector", "s_v")
        self.wait("scalar", "s_v", v1)
        self.emit("scalar", lambda e, xap=xap, sq=sq: e.activation(
            out=sq, in_=xap, func=AF.Square, bias=st[:, 1:2], scale=1.0))
        a1 = self.inc_tag("scalar", "s_a")
        self.wait("vector", "s_a", a1)
        self.emit("vector", lambda e, sq=sq: e.reduce_sum(st[:, 3:4], sq, axis=AX.X))
        v1 = self.inc_tag("vector", "s_v")
        self.wait("scalar", "s_v", v1)
        self.emit("scalar", lambda e: e.activation(
            out=st[:, 4:5], in_=st[:, 3:4], func=AF.Sqrt, bias=st[:, 16:17],
            scale=1.0 / c.DM))
        a1 = self.inc_tag("scalar", "s_a")
        self.wait("vector", "s_a", a1)
        self.emit("vector", lambda e: e.reciprocal(st[:, 5:6], st[:, 4:5]))
        self.inc_tag("vector", "s_v")
        self.emit("vector", lambda e, xap=xap, out_tm=out_tm: e.tensor_scalar(
            out_tm, xap, st[:, 2:3], st[:, 5:6], ALU.subtract, ALU.mult))
        vnorm = self.inc_tag("vector", "s_v")
        if to_T is not None:
            th, tl = to_T
            self.wait("tensor", "s_v", vnorm)
            self.transpose_block_hl(
                out_tm, lambda ch, ti=ti: (th[:, ch, 128 * ti:128 * (ti + 1)],
                                           tl[:, ch, 128 * ti:128 * (ti + 1)]),
                banks=(4, 5, 6, 7))
        return vnorm

    def phase4(self):
        c = self.cfg
        self.wait("sync", "s_t", self.cnt["s_t"])
        self.wait("sync", "s_v", self.cnt["s_v"])
        self.wait("sync", "s_a", self.cnt["s_a"])
        self.wait("gpsimd", "s_t", self.cnt["s_t"])
        self.wait("gpsimd", "s_v", self.cnt["s_v"])
        for rc in range(2):
            gw = self.dma("gw", lambda e, rc=rc: e.dma_start(
                out=self.ytile[:, rc, :], in_=self.d_v2[128 * rc:128 * (rc + 1), :]),
                eng="gpsimd")
            self.wait_g("vector", gw)
            self.emit("vector", lambda e, rc=rc: e.tensor_copy(
                self.v2sh[:, rc, :], self.ytile[:, rc, :]))
            self.inc_tag("vector", "s_v")
            self.emit("vector", lambda e, rc=rc: e.tensor_tensor(
                self.v2sl[:, rc, :], self.ytile[:, rc, :], self.v2sh[:, rc, :],
                ALU.subtract))
            self.inc_tag("vector", "s_v")
        u1_guard = [None, None]
        for wc in range(c.CCH):
            k = wc % 2
            stg = self.wsplit[:, 256 * k:256 * k + 256]
            self.wait_g("sync", u1_guard[k])
            gw = self.dma("in", lambda e, wc=wc, stg=stg: e.dma_start(
                out=stg, in_=self.d_u1[128 * wc:128 * (wc + 1), :]))
            self.wait_g("vector", gw)
            self.emit("vector", lambda e, wc=wc, stg=stg: e.tensor_copy(
                self.u1sh[:, wc, :], stg))
            self.inc_tag("vector", "s_v")
            self.emit("vector", lambda e, wc=wc, stg=stg: e.tensor_tensor(
                self.u1sl[:, wc, :], stg, self.u1sh[:, wc, :],
                ALU.subtract))
            vsp = self.inc_tag("vector", "s_v")
            u1_guard[k] = ("s_v", vsp)
        self.dma("in", lambda e: e.dma_start(out=self.b1s, in_=self.d_b1))
        g_b2 = self.dma("in", lambda e: e.dma_start(out=self.b2s[0:1, :], in_=self.d_b2))
        self.wait_g("vector", g_b2)
        self.emit("vector", lambda e: e.tensor_copy(self.b2s16[0:1, 0:1024], self.b2s[0:1, :]))
        self.inc_tag("vector", "s_v")
        self.emit("vector", lambda e: e.tensor_tensor(
            self.b2s16[0:1, 1024:2048], self.b2s[0:1, :], self.b2s16[0:1, 0:1024],
            ALU.subtract))
        self.inc_tag("vector", "s_v")
        self.wait_stream("tensor", "in")
        self.wait("tensor", "s_v", self.cnt["s_v"])

        st_guard = [None, None]
        yt_guard = [None, None]
        for half in range(c.OWN // 1024):
            for tcl in range(2):
                tc = 2 * half + tcl
                for rc in range(2):
                    bi = (2 * tcl + rc) % 4
                    bank = self.pb[bi]
                    self.wait_g("tensor", self.bguard[bi])
                    for ch in range(c.CCH):
                        self.mm3(bank[:, :],
                                 self.u1sh[:, ch, 128 * rc:128 * (rc + 1)],
                                 self.u1sl[:, ch, 128 * rc:128 * (rc + 1)],
                                 self.x1Th[:, ch, 512 * tc:512 * (tc + 1)],
                                 self.x1Tl[:, ch, 512 * tc:512 * (tc + 1)],
                                 start=(ch == 0), stop=(ch == c.CCH - 1))
                    self.wait("scalar", "s_t", self.cnt["s_t"])
                    _, v = self.evac_b(bank[:, :],
                                       self.pth[:, rc, 512 * tcl:512 * (tcl + 1)],
                                       self.ptl[:, rc, 512 * tcl:512 * (tcl + 1)])
                    self.bguard[bi] = ("s_v", v)
            def ld_split(fc):
                dbi = fc % 2
                self.wait_g("sync", st_guard[dbi])
                gv = self.dma("in", lambda e, fc=fc, dbi=dbi: e.dma_start(
                    out=self.v1s32[:, dbi, :].rearrange("p (a b) -> p a b", a=2),
                    in_=self.d_v1[:, 128 * fc:128 * (fc + 1)].rearrange(
                        "(a p) b -> p a b", p=128)))
                gu = self.dma("in", lambda e, fc=fc, dbi=dbi: e.dma_start(
                    out=self.u2s32[:, dbi, :], in_=self.d_u2[128 * fc:128 * (fc + 1), :]))
                self.wait_g("vector", gv)
                self.wait_g("vector", gu)
                for (s32, sh, sl) in ((self.v1s32, self.v1sth, self.v1stl),
                                      (self.u2s32, self.u2sth, self.u2stl)):
                    self.emit("vector", lambda e, s32=s32, sh=sh, dbi=dbi: e.tensor_copy(
                        sh[:, dbi, :], s32[:, dbi, :]))
                    self.inc_tag("vector", "s_v")
                    self.emit("vector", lambda e, s32=s32, sh=sh, sl=sl, dbi=dbi:
                              e.tensor_tensor(sl[:, dbi, :], s32[:, dbi, :],
                                              sh[:, dbi, :], ALU.subtract))
                    self.inc_tag("vector", "s_v")
                return self.cnt["s_v"]

            gtmp_guard = getattr(self, "_gtmp_guard", [None, None])
            self._gtmp_guard = gtmp_guard
            vsplit_map = {}
            vsplit_map[0] = ld_split(0)
            vsplit_map[1] = ld_split(1)
            for fc in range(c.FC):
                dbi = fc % 2
                self.wait("tensor", "s_v", vsplit_map[fc])
                last_mm = 0
                tmm = {}
                for tc2 in range(2):
                    hbi = 2 + tc2
                    self.wait_g("tensor", self.bguard[hbi])
                    self.mm3(self.pb[hbi][:, 0:512],
                             self.v1sth[:, dbi, 0:128], self.v1stl[:, dbi, 0:128],
                             self.pth[:, 0, 512 * tc2:512 * (tc2 + 1)],
                             self.ptl[:, 0, 512 * tc2:512 * (tc2 + 1)],
                             start=True, stop=False)
                    tmm[tc2] = self.mm3(
                        self.pb[hbi][:, 0:512],
                        self.v1sth[:, dbi, 128:256], self.v1stl[:, dbi, 128:256],
                        self.pth[:, 1, 512 * tc2:512 * (tc2 + 1)],
                        self.ptl[:, 1, 512 * tc2:512 * (tc2 + 1)],
                        start=False, stop=True)
                ag = {}
                for tc2 in range(2):
                    hbi = 2 + tc2
                    self.wait("scalar", "s_t", tmm[tc2])
                    self.emit("scalar", lambda e, hbi=hbi, fc=fc, tc2=tc2:
                              e.activation(out=self.htsh[:, tc2, :],
                                           in_=self.pb[hbi][:, 0:512],
                                           func=AF.Gelu_apprx_tanh,
                                           bias=self.b1s[:, fc:fc + 1], scale=1.0))
                    self.inc_tag("scalar", "s_a")
                    self.wait_g("scalar", gtmp_guard[tc2])
                    self.emit("scalar", lambda e, hbi=hbi, fc=fc, tc2=tc2:
                              e.activation(out=self.gtmp2[:, tc2, :],
                                           in_=self.pb[hbi][:, 0:512],
                                           func=AF.Gelu_apprx_tanh,
                                           bias=self.b1s[:, fc:fc + 1], scale=1.0))
                    ag[tc2] = self.inc_tag("scalar", "s_a")
                    self.bguard[hbi] = ("s_a", ag[tc2])
                vgl = {}
                for tc2 in range(2):
                    self.wait("vector", "s_a", ag[tc2])
                    self.emit("vector", lambda e, tc2=tc2: e.tensor_tensor(
                        self.htsl[:, tc2, :], self.gtmp2[:, tc2, :],
                        self.htsh[:, tc2, :], ALU.subtract))
                    vgl[tc2] = self.inc_tag("vector", "s_v")
                    gtmp_guard[tc2] = ("s_v", vgl[tc2])
                for tc2 in range(2):
                    self.wait("tensor", "s_a", ag[tc2])
                    self.wait("tensor", "s_v", vgl[tc2])
                    for r2 in range(2):
                        qbi = 4 + 2 * tc2 + r2
                        if fc == 0:
                            self.wait_g("tensor", self.bguard[qbi])
                        last_mm = self.mm3(
                            self.pb[qbi][:, :],
                            self.u2sth[:, dbi, 128 * r2:128 * (r2 + 1)],
                            self.u2stl[:, dbi, 128 * r2:128 * (r2 + 1)],
                            self.htsh[:, tc2, :], self.htsl[:, tc2, :],
                            start=(fc == 0), stop=(fc == c.FC - 1))
                st_guard[dbi] = ("s_t", last_mm)
                if fc + 2 < c.FC:
                    vsplit_map[fc + 2] = ld_split(fc + 2)
            for tc2 in range(2):
                self.wait("scalar", "s_t", self.cnt["s_t"])
                vev = 0
                for r2 in range(2):
                    qbi = 4 + 2 * tc2 + r2
                    _, vev = self.evac_b(self.pb[qbi][:, :],
                                         self.q2th[:, r2, :], self.q2tl[:, r2, :])
                    self.bguard[qbi] = ("s_v", vev)
                for tt in range(4):
                    ti = 8 * half + 4 * tc2 + tt
                    ydbi = ti % 2
                    for dk in range(2):
                        ybi = (2 * tt + dk) % 4
                        ybank = self.pb[ybi]
                        self.wait_g("tensor", self.bguard[ybi])
                        self.wait("tensor", "s_v", vev)
                        for r2 in range(2):
                            self.mm3(ybank[:, :],
                                     self.q2th[:, r2, 128 * tt:128 * (tt + 1)],
                                     self.q2tl[:, r2, 128 * tt:128 * (tt + 1)],
                                     self.v2sh[:, r2, 512 * dk:512 * (dk + 1)],
                                     self.v2sl[:, r2, 512 * dk:512 * (dk + 1)],
                                     start=(r2 == 0), stop=False)
                        self.emit("tensor", lambda e, ybank=ybank, dk=dk:
                                  e.matmul(ybank[:, :], self.ones16[0:1, :],
                                           self.b2s16[0:1, 512 * dk:512 * (dk + 1)],
                                           start=False, stop=False))
                        self.emit("tensor", lambda e, ybank=ybank, dk=dk:
                                  e.matmul(ybank[:, :], self.ones16[0:1, :],
                                           self.b2s16[0:1, 1024 + 512 * dk:1024 + 512 * (dk + 1)],
                                           start=False, stop=True))
                        tmm = self.inc_tag("tensor", "s_t")
                        self.wait("vector", "s_t", tmm)
                        self.emit("vector", lambda e, ybank=ybank, dk=dk, ti=ti, lnx=self.lnx4:
                                  e.tensor_tensor(lnx[:, 512 * dk:512 * (dk + 1)],
                                                  ybank[:, :],
                                                  self.x1tmh[:, ti, 512 * dk:512 * (dk + 1)],
                                                  ALU.add))
                        v = self.inc_tag("vector", "s_v")
                        self.bguard[ybi] = ("s_v", v)
                        self.emit("vector", lambda e, dk=dk, ti=ti, lnx=self.lnx4:
                                  e.tensor_tensor(lnx[:, 512 * dk:512 * (dk + 1)],
                                                  lnx[:, 512 * dk:512 * (dk + 1)],
                                                  self.x1tml[:, ti, 512 * dk:512 * (dk + 1)],
                                                  ALU.add))
                        self.inc_tag("vector", "s_v")
                    self.wait_g("vector", yt_guard[ydbi])
                    vln = self.layernorm(self.lnx4, self.ytile[:, ydbi, :], ti, None,
                                         sq=self.lnsq4)
                    self.wait("sync", "s_v", vln)
                    yt_guard[ydbi] = self.dma("y", lambda e, ti=ti, ydbi=ydbi: e.dma_start(
                        out=self.d_y[128 * ti:128 * (ti + 1), :], in_=self.ytile[:, ydbi, :]))
        self.wait_stream("sync", "y")


def _stage_core(inputs, cfg, b, r):
    c = cfg
    f32 = np.float32
    x = np.asarray(inputs["L0_attn_data"], dtype=f32)
    xb = np.ascontiguousarray(x[b])
    tiles = xb.reshape(c.NT, 128, c.DM)
    perm = np.arange(c.NT)
    if r == 1:
        perm = perm.reshape(-1, 2)[:, ::-1].ravel()
    x_full = tiles[perm].reshape(c.T, c.DM)
    x_fullT = np.ascontiguousarray(x_full.T)
    x_own = np.ascontiguousarray(tiles[perm[0::2]].reshape(c.OWN, c.DM))

    qkv_b = np.asarray(inputs["L0_attn_qkv_b"], dtype=f32)
    b1 = np.asarray(inputs["L0_ffn_b1"], dtype=f32)
    import ml_dtypes
    bf16 = np.dtype(ml_dtypes.bfloat16)
    D = np.where(np.arange(128)[:, None] <= np.arange(128)[None, :],
                 f32(0.0), f32(NEG_BIG)).astype(bf16)
    m1 = (np.full((128, 128), NEG_BIG, f32) if r == 0
          else np.zeros((128, 128), f32)).astype(bf16)
    return {
        "x_fullT": x_fullT,
        "x_own": x_own,
        "qkv_wT": np.ascontiguousarray(np.asarray(inputs["L0_attn_qkv_w"], f32).T),
        "qkv_b2d": np.ascontiguousarray(qkv_b.reshape(-1, 128).T),
        "out_wT": np.ascontiguousarray(np.asarray(inputs["L0_attn_out_w"], f32).T),
        "out_b2d": np.ascontiguousarray(np.asarray(inputs["L0_attn_out_b"], f32).reshape(1, -1)),
        "U1": np.ascontiguousarray(np.asarray(inputs["L0_ffn_U1"], f32)),
        "V1": np.ascontiguousarray(np.asarray(inputs["L0_ffn_V1"], f32)),
        "b1_2d": np.ascontiguousarray(b1.reshape(-1, 128).T),
        "U2": np.ascontiguousarray(np.asarray(inputs["L0_ffn_U2"], f32)),
        "V2": np.ascontiguousarray(np.asarray(inputs["L0_ffn_V2"], f32)),
        "b2_2d": np.ascontiguousarray(np.asarray(inputs["L0_ffn_b2"], f32).reshape(1, -1)),
        "c_ident": np.eye(128, dtype=f32),
        "c_masks": np.ascontiguousarray(np.concatenate([D, D, m1, m1], axis=1)),
    }


def _build(cfg):
    return B(cfg).build()


def kernel(**inputs):
    cfg = Cfg()
    nc = _build(cfg)
    in_maps = []
    meta = []
    for core in range(8):
        b, r = core // 2, core % 2
        in_maps.append(_stage_core(inputs, cfg, b, r))
        meta.append((b, r))
    res = run_bass_kernel_spmd(nc, in_maps, core_ids=list(range(8)))
    y = np.empty((4, cfg.T, cfg.DM), dtype=np.float32)
    for core, (b, r) in enumerate(meta):
        yo = np.asarray(res.results[core]["y_own"], dtype=np.float32)
        yv = y[b].reshape(cfg.NT, 128, cfg.DM)
        perm = np.arange(cfg.NT)
        if r == 1:
            perm = perm.reshape(-1, 2)[:, ::-1].ravel()
        yv[perm[0::2]] = yo.reshape(cfg.NT_OWN, 128, cfg.DM)
    return y
